# version 02
# baseline (speedup 1.0000x reference)
import numpy as np
import ml_dtypes
import concourse.bass as bass
import concourse.mybir as mybir
from concourse.bass_utils import run_bass_kernel_spmd

F32 = mybir.dt.float32
BF16 = mybir.dt.bfloat16
AF = mybir.ActivationFunctionType
ALU = mybir.AluOpType
AX = mybir.AxisListType.X
NBF = np.dtype(ml_dtypes.bfloat16)

T, D, OWN = 8192, 1024, 2048
ENGS = ("sync", "gpsimd", "vector", "scalar", "tensor")
L0PERM = [h for o in range(8) for h in (o, o + 8)]
PERM_COLS = np.concatenate([np.arange(h * 64, (h + 1) * 64) for h in L0PERM])
CLS_OFF = [0, 16, 48, 96]


class Prog:
    def __init__(self):
        self.nc = bass.Bass("TRN2", target_bir_lowering=False)
        self.ops = []
        self._keep = []

    def din(self, name, shape, dtype=F32):
        return self.nc.dram_tensor(name, shape, dtype, kind="ExternalInput").ap()

    def dout(self, name, shape, dtype=F32):
        return self.nc.dram_tensor(name, shape, dtype, kind="ExternalOutput").ap()

    def dint(self, name, shape, dtype):
        return self.nc.dram_tensor(name, shape, dtype, kind="Internal").ap()

    def sb(self, name, shape, dtype):
        return self.nc.alloc_sbuf_tensor(name, shape, dtype)

    def psum(self, name, shape, dtype=F32):
        cm = self.nc.psum_tensor(name, shape, dtype)
        t = cm.__enter__()
        self._keep.append(cm)
        return t

    def _e(self, eng, meth, args, kw):
        self.ops.append(
            (eng, lambda e, m=meth, a=args, k=kw: getattr(e, m)(*a, **k), 1))

    def v(self, meth, *args, **kw):
        self._e("vector", meth, args, kw)

    def s(self, meth, *args, **kw):
        self._e("scalar", meth, args, kw)

    def dma(self, out, in_, q="sync"):
        self.ops.append(
            (q, lambda e, o=out, i=in_: e.dma_start(out=o, in_=i), 16))

    def mm(self, items):
        def fn(e, items=tuple(items)):
            for (o, l, r, st, sp) in items:
                ins = e.matmul(o, l, r, start=st, stop=sp)
            return ins
        self.ops.append(("tensor", fn, 1))

    def finish(self):
        cm = self.nc.semaphore("g")
        g = cm.__enter__()
        self._keep.append(cm)
        sched = []
        tot = 0
        for eng, fn, inc in self.ops:
            sched.append((eng, fn, inc, tot))
            tot += inc
        with self.nc.Block() as blk:
            for engname in ENGS:
                @getattr(blk, engname)
                def _(e, engname=engname):
                    for eng, fn, inc, before in sched:
                        if eng != engname:
                            continue
                        if before:
                            e.wait_ge(g, before)
                        fn(e).then_inc(g, inc)
                    e.wait_ge(g, tot)
        return self


def _ln_stats(p, xt, xc, sq, st, cst):
    p.v("reduce_sum", st[:, 0:1], xt[:], AX)
    p.v("tensor_scalar_mul", st[:, 1:2], st[:, 0:1], 1.0 / D)
    p.v("tensor_scalar", xc[:], xt[:], st[:, 1:2], None, ALU.subtract)
    p.s("activation", sq[:], xc[:], AF.Square, accum_out=st[:, 2:3])
    p.s("activation", st[:, 3:4], st[:, 2:3], AF.Ln,
        scale=cst[:, 0:1], bias=cst[:, 1:2])
    p.s("activation", st[:, 3:4], st[:, 3:4], AF.Exp, scale=cst[:, 2:3])
    p.v("tensor_scalar_mul", xc[:], xc[:], st[:, 3:4])


def _cast_rows(p, dst_slab, dst_col0, src_dram, r0, cols, stage, bf_w=None):
    p.dma(stage[:, 0:cols], src_dram[r0:r0 + 128, 0:cols])
    p.v("tensor_copy", dst_slab[:, dst_col0:dst_col0 + cols], stage[:, 0:cols])


def build_ln():
    p = Prog()
    xin = p.din("x", [OWN, D])
    xout = p.dout("xn", [OWN, D])
    xt = p.sb("xt", [128, D], F32)
    xc = p.sb("xc", [128, D], F32)
    sq = p.sb("sq", [128, D], F32)
    st = p.sb("st", [128, 4], F32)
    cst = p.sb("cst", [128, 4], F32)
    p.v("memset", cst[:, 0:1], 1.0 / D)
    p.v("memset", cst[:, 1:2], 1e-5)
    p.v("memset", cst[:, 2:3], -0.5)
    for t in range(OWN // 128):
        p.dma(xt[:], xin[t * 128:(t + 1) * 128, :])
        _ln_stats(p, xt, xc, sq, st, cst)
        p.dma(xout[t * 128:(t + 1) * 128, :], xc[:], q="gpsimd")
    return p


def build_l0ffn():
    p = Prog()
    xnT = p.din("xnT", [D, OWN])
    pcol = p.din("pcol", [128, 8])
    WrT = p.din("WrT", [D, D])
    WkT = p.din("WkT", [D, D])
    WvT = p.din("WvT", [D, D])
    mur_d = p.din("mur", [128, 8])
    muk_d = p.din("muk", [128, 8])
    gout = p.dout("gT", [D, OWN])

    xnc = p.sb("xnc", [128, OWN + 1], F32)
    pcs = p.sb("pcs", [128, 8], F32)
    mur = p.sb("murs", [128, 8], F32)
    muk = p.sb("muks", [128, 8], F32)
    m1r = p.sb("m1r", [128, 8], F32)
    m1k = p.sb("m1k", [128, 8], F32)
    mixt = p.sb("mixt", [128, OWN], F32)
    xr = p.sb("xr", [128, 8 * OWN], BF16)
    xk = p.sb("xk", [128, 8 * OWN], BF16)
    rf = p.sb("rf", [128, 8 * OWN], F32)
    w = p.sb("w", [128, 8 * 1024], BF16)
    wst = p.sb("wst", [128, 1024], F32)
    kr = p.sb("kr", [128, 512], F32)
    gt = p.sb("gt", [128, 512], F32)
    pa = p.psum("pa", [128, 512], F32)

    p.dma(pcs[:], pcol[:, :])
    p.dma(mur[:], mur_d[:, :])
    p.dma(muk[:], muk_d[:, :])
    p.v("tensor_scalar", m1r[:], mur[:], -1.0, 1.0, ALU.mult, ALU.add)
    p.v("tensor_scalar", m1k[:], muk[:], -1.0, 1.0, ALU.mult, ALU.add)

    for i in range(8):
        p.dma(xnc[:, 1:OWN + 1], xnT[i * 128:(i + 1) * 128, :])
        p.v("tensor_copy", xnc[:, 0:1], pcs[:, i:i + 1])
        own = xnc[:, 1:OWN + 1]
        prev = xnc[:, 0:OWN]
        p.v("tensor_scalar_mul", mixt[:], prev, m1r[:, i:i + 1])
        p.v("scalar_tensor_tensor", xr[:, i * OWN:(i + 1) * OWN], own,
            mur[:, i:i + 1], mixt[:], ALU.mult, ALU.add)
        p.v("tensor_scalar_mul", mixt[:], prev, m1k[:, i:i + 1])
        p.v("scalar_tensor_tensor", xk[:, i * OWN:(i + 1) * OWN], own,
            muk[:, i:i + 1], mixt[:], ALU.mult, ALU.add)

    def load_w(src):
        for i in range(8):
            _cast_rows(p, w, i * 1024, src, i * 128, 1024, wst)

    def proj(act, oc, tt, width=512):
        return [(pa[:, 0:width],
                 w[:, i * 1024 + oc * 128: i * 1024 + (oc + 1) * 128],
                 act[:, i * OWN + tt * 512: i * OWN + tt * 512 + width],
                 i == 0, i == 7) for i in range(8)]

    load_w(WrT)
    for oc in range(8):
        for tt in range(4):
            p.mm(proj(xr, oc, tt))
            p.s("activation", rf[:, oc * OWN + tt * 512: oc * OWN + tt * 512 + 512],
                pa[:, 0:512], AF.Sigmoid)
    load_w(WkT)
    for oc in range(8):
        for tt in range(4):
            p.mm(proj(xk, oc, tt))
            p.s("activation", kr[:], pa[:, 0:512], AF.Relu)
            p.s("activation", xr[:, oc * OWN + tt * 512: oc * OWN + tt * 512 + 512],
                kr[:], AF.Square)
    load_w(WvT)
    for oc in range(8):
        for tt in range(4):
            p.mm(proj(xr, oc, tt))
            p.v("tensor_mul", gt[:], pa[:, 0:512],
                rf[:, oc * OWN + tt * 512: oc * OWN + tt * 512 + 512])
            p.dma(gout[oc * 128:(oc + 1) * 128, tt * 512:tt * 512 + 512],
                  gt[:], q="gpsimd")
    return p


def build_l0att():
    HALO = OWN + 256
    p = Prog()
    xh_d = p.din("xh", [D, HALO])
    wq_d = p.din("wqp", [D, D])
    bq_d = p.din("bqp", [128, 8])
    wk_d = p.din("wk", [D, 128])
    bk_d = p.din("bk", [128, 1])
    wv_d = p.din("wv", [D, 128])
    bv_d = p.din("bv", [1, 128])
    km_d = p.din("kmask", [128, HALO])
    vm_d = p.din("vmask", [128, 18])
    bd_d = p.din("band", [128, 384], BF16)
    atm = p.dout("atm", [OWN, D])

    xh = p.sb("xhs", [128, 8 * HALO], BF16)
    qT = p.sb("qT", [128, 8 * OWN], BF16)
    kT = p.sb("kT", [128, HALO], BF16)
    vas = p.sb("vas", [128, 18 * 130], BF16)
    va = vas[:, :].rearrange("p (n c) -> p n c", c=130)
    wq = p.sb("wq", [128, 8 * 1024], BF16)
    wkv = p.sb("wkv", [128, 8 * 128], BF16)
    wst = p.sb("wst", [128, 1024], F32)
    xst = p.sb("xst", [128, HALO], F32)
    km = p.sb("km", [128, HALO], F32)
    vm = p.sb("vm", [128, 18], F32)
    bd = p.sb("bd", [128, 384], BF16)
    bqs = p.sb("bqs", [128, 8], F32)
    bks = p.sb("bks", [128, 1], F32)
    bvr = p.sb("bvr", [1, 128], F32)
    ones = p.sb("ones", [1, 128], F32)
    bvb = p.sb("bvb", [128, 128], F32)
    ktmp = p.sb("ktmp", [128, 512], F32)
    vt1 = p.sb("vt1", [128, 128], F32)
    vt2 = p.sb("vt2", [128, 128], F32)
    pp = p.sb("pp", [128, 384], BF16)
    pm = p.sb("pm", [128, 384], BF16)
    rl = p.sb("rl", [128, 1], F32)
    otm = p.sb("otm", [128, D], F32)
    pa = p.psum("pa", [128, 512], F32)
    pv = p.psum("pv", [128, 128], F32)
    sps = p.psum("sps", [128, 384], F32)
    ops_ = p.psum("ops", [128, 65], F32)

    p.dma(km[:], km_d[:, :])
    p.dma(vm[:], vm_d[:, :])
    p.dma(bd[:], bd_d[:, :])
    p.dma(bqs[:], bq_d[:, :])
    p.dma(bks[:], bk_d[:, :])
    p.dma(bvr[:], bv_d[:, :])
    p.v("memset", ones[:], 1.0)
    p.v("memset", va[:, :, 64:65], 1.0)
    p.v("memset", va[:, :, 129:130], 1.0)
    p.mm([(pa[:, 0:128], ones[:], bvr[:], True, True)])
    p.v("tensor_copy", bvb[:], pa[:, 0:128])
    for i in range(8):
        _cast_rows(p, xh, i * HALO, xh_d, i * 128, HALO, xst)
    for i in range(8):
        _cast_rows(p, wq, i * 1024, wq_d, i * 128, 1024, wst)
    for oc in range(8):
        for tt in range(4):
            p.mm([(pa[:, 0:512],
                   wq[:, i * 1024 + oc * 128: i * 1024 + (oc + 1) * 128],
                   xh[:, i * HALO + 128 + tt * 512: i * HALO + 128 + tt * 512 + 512],
                   i == 0, i == 7) for i in range(8)])
            p.s("activation", qT[:, oc * OWN + tt * 512: oc * OWN + tt * 512 + 512],
                pa[:, 0:512], AF.Identity, bias=bqs[:, oc:oc + 1])
    for i in range(8):
        _cast_rows(p, wkv, i * 128, wk_d, i * 128, 128, wst)
    for t0, wd in ((0, 512), (512, 512), (1024, 512), (1536, 512), (2048, 256)):
        p.mm([(pa[:, 0:wd], wkv[:, i * 128:(i + 1) * 128],
               xh[:, i * HALO + t0: i * HALO + t0 + wd], i == 0, i == 7)
              for i in range(8)])
        p.s("activation", ktmp[:, 0:wd], pa[:, 0:wd], AF.Identity,
            bias=bks[:, 0:1])
        p.v("tensor_mul", kT[:, t0:t0 + wd], ktmp[:, 0:wd], km[:, t0:t0 + wd])
    for i in range(8):
        _cast_rows(p, wkv, i * 128, wv_d, i * 128, 128, wst)
    for tt in range(18):
        p.mm([(pv[:, 0:128], xh[:, i * HALO + tt * 128: i * HALO + (tt + 1) * 128],
               wkv[:, i * 128:(i + 1) * 128], i == 0, i == 7) for i in range(8)])
        p.v("tensor_tensor", vt1[:], pv[:, 0:128], bvb[:], ALU.add)
        p.v("tensor_scalar_mul", vt2[:], vt1[:], vm[:, tt:tt + 1])
        p.v("tensor_copy", va[:, tt, 0:64], vt2[:, 0:64])
        p.v("tensor_copy", va[:, tt, 65:129], vt2[:, 64:128])
    for qb in range(16):
        for o in range(8):
            for half in range(2):
                h = o if half == 0 else o + 8
                pb = half * 64
                p.mm([(sps[:, kbi * 128:(kbi + 1) * 128],
                       kT[pb:pb + 64, (qb + kbi) * 128:(qb + kbi + 1) * 128],
                       qT[pb:pb + 64, o * OWN + qb * 128: o * OWN + (qb + 1) * 128],
                       True, True) for kbi in range(3)])
                p.s("activation", pp[:], sps[:], AF.Exp)
                p.v("tensor_mul", pm[:], pp[:], bd[:])
                p.mm([(ops_[:, 0:65], pm[:, kbi * 128:(kbi + 1) * 128],
                       va[:, qb + kbi, half * 65:(half + 1) * 65],
                       kbi == 0, kbi == 2) for kbi in range(3)])
                p.v("reciprocal", rl[:], ops_[:, 64:65])
                p.v("tensor_scalar_mul", otm[:, h * 64:(h + 1) * 64],
                    ops_[:, 0:64], rl[:, 0:1])
        p.dma(atm[qb * 128:(qb + 1) * 128, :], otm[:], q="gpsimd")
    return p


def build_l0cmb():
    p = Prog()
    aT_d = p.din("aT", [D, OWN])
    wo_d = p.din("wo", [D, D])
    bo_d = p.din("bo", [128, 8])
    gT_d = p.din("gT", [D, OWN])
    xT_d = p.din("xT", [D, OWN])
    x1T = p.dout("x1T", [D, OWN])

    ab = p.sb("ab", [128, 8 * OWN], BF16)
    w = p.sb("w", [128, 8 * 1024], BF16)
    wst = p.sb("wst", [128, 1024], F32)
    stg = p.sb("stg", [128, OWN], F32)
    bos = p.sb("bos", [128, 8], F32)
    stmp = p.sb("stmp", [128, 512], F32)
    gst = p.sb("gst", [128, 512], F32)
    xst = p.sb("xst", [128, 512], F32)
    ot = p.sb("ot", [128, 512], F32)
    pa = p.psum("pa", [128, 512], F32)

    p.dma(bos[:], bo_d[:, :])
    for i in range(8):
        _cast_rows(p, ab, i * OWN, aT_d, i * 128, OWN, stg)
    for i in range(8):
        _cast_rows(p, w, i * 1024, wo_d, i * 128, 1024, wst)
    for oc in range(8):
        for tt in range(4):
            p.mm([(pa[:, 0:512],
                   w[:, i * 1024 + oc * 128: i * 1024 + (oc + 1) * 128],
                   ab[:, i * OWN + tt * 512: i * OWN + tt * 512 + 512],
                   i == 0, i == 7) for i in range(8)])
            p.s("activation", stmp[:], pa[:, 0:512], AF.Identity,
                bias=bos[:, oc:oc + 1])
            p.dma(gst[:], gT_d[oc * 128:(oc + 1) * 128, tt * 512:tt * 512 + 512])
            p.s("activation", gst[:], gst[:], AF.Silu)
            p.dma(xst[:], xT_d[oc * 128:(oc + 1) * 128, tt * 512:tt * 512 + 512])
            p.v("tensor_mul", ot[:], stmp[:], gst[:])
            p.v("tensor_tensor", ot[:], ot[:], xst[:], ALU.add)
            p.dma(x1T[oc * 128:(oc + 1) * 128, tt * 512:tt * 512 + 512],
                  ot[:], q="gpsimd")
    return p


def build_proj1():
    p = Prog()
    aT_d = p.din("aT", [D, OWN])
    wp_d = p.din("wp", [D, D])
    bp_d = p.din("bp", [128, 8])
    xnT_d = p.din("xnT", [D, OWN])
    y1T = p.dout("y1T", [D, OWN])

    ab = p.sb("ab", [128, 8 * OWN], BF16)
    w = p.sb("w", [128, 8 * 1024], BF16)
    wst = p.sb("wst", [128, 1024], F32)
    stg = p.sb("stg", [128, OWN], F32)
    bps = p.sb("bps", [128, 8], F32)
    stmp = p.sb("stmp", [128, 512], F32)
    xst = p.sb("xst", [128, 512], F32)
    ot = p.sb("ot", [128, 512], F32)
    pa = p.psum("pa", [128, 512], F32)

    p.dma(bps[:], bp_d[:, :])
    for i in range(8):
        _cast_rows(p, ab, i * OWN, aT_d, i * 128, OWN, stg)
    for i in range(8):
        _cast_rows(p, w, i * 1024, wp_d, i * 128, 1024, wst)
    for oc in range(8):
        for tt in range(4):
            p.mm([(pa[:, 0:512],
                   w[:, i * 1024 + oc * 128: i * 1024 + (oc + 1) * 128],
                   ab[:, i * OWN + tt * 512: i * OWN + tt * 512 + 512],
                   i == 0, i == 7) for i in range(8)])
            p.s("activation", stmp[:], pa[:, 0:512], AF.Identity,
                bias=bps[:, oc:oc + 1])
            p.dma(xst[:], xnT_d[oc * 128:(oc + 1) * 128, tt * 512:tt * 512 + 512])
            p.v("tensor_tensor", ot[:], stmp[:], xst[:], ALU.add)
            p.dma(y1T[oc * 128:(oc + 1) * 128, tt * 512:tt * 512 + 512],
                  ot[:], q="gpsimd")
    return p


def build_qkv1():
    p = Prog()
    xnT_d = p.din("xnT", [D, OWN])
    wq_d = p.din("wq1", [D, D])
    wk_d = p.din("wk1", [D, D])
    wv_d = p.din("wv1", [D, D])
    wr_d = p.din("wr", [128, 64])
    br_d = p.din("br", [1, 8])
    qT_o = p.dout("qT", [D, OWN])
    kT_o = p.dout("kT", [D, OWN])
    vT_o = p.dout("vT", [D, OWN])
    msk_o = p.dout("msk", [OWN, 8])
    mp_o = p.dout("mp", [OWN, 1])

    xb = p.sb("xb", [128, 8 * OWN], BF16)
    xf = p.sb("xf", [128, 8 * OWN], F32)
    w = p.sb("w", [128, 8 * 1024], BF16)
    wst = p.sb("wst", [128, 1024], F32)
    wr = p.sb("wrs", [128, 64], F32)
    brs = p.sb("brs", [1, 8], F32)
    ones = p.sb("ones", [1, 128], F32)
    brb = p.sb("brb", [128, 8], F32)
    rt = p.sb("rt", [128, 8], F32)
    mps = p.sb("mps", [128, 1], F32)
    mks = p.sb("mks", [128, 8], F32)
    ot = p.sb("ot", [128, 512], F32)
    pa = p.psum("pa", [128, 512], F32)
    pr = p.psum("pr", [128, 128], F32)

    p.dma(wr[:], wr_d[:, :])
    p.dma(brs[:], br_d[:, :])
    p.v("memset", ones[:], 1.0)
    p.mm([(pa[:, 0:8], ones[:], brs[:], True, True)])
    p.v("tensor_copy", brb[:], pa[:, 0:8])
    for i in range(8):
        p.dma(xf[:, i * OWN:(i + 1) * OWN], xnT_d[i * 128:(i + 1) * 128, :])
        p.v("tensor_copy", xb[:, i * OWN:(i + 1) * OWN],
            xf[:, i * OWN:(i + 1) * OWN])
    for tt in range(16):
        p.mm([(pr[:, 0:8], xf[:, i * OWN + tt * 128: i * OWN + (tt + 1) * 128],
               wr[:, i * 8:(i + 1) * 8], i == 0, i == 7) for i in range(8)])
        p.v("tensor_tensor", rt[:], pr[:, 0:8], brb[:], ALU.add)
        p.s("activation", rt[:], rt[:], AF.Sigmoid)
        p.v("reduce_max", mps[:], rt[:], AX)
        p.v("tensor_scalar", mks[:], rt[:], mps[:, 0:1], None, ALU.is_ge)
        p.dma(msk_o[tt * 128:(tt + 1) * 128, :], mks[:], q="gpsimd")
        p.dma(mp_o[tt * 128:(tt + 1) * 128, :], mps[:], q="gpsimd")
    for (wsrc, dst) in ((wq_d, qT_o), (wk_d, kT_o), (wv_d, vT_o)):
        for i in range(8):
            _cast_rows(p, w, i * 1024, wsrc, i * 128, 1024, wst)
        for oc in range(8):
            for tt in range(4):
                p.mm([(pa[:, 0:512],
                       w[:, i * 1024 + oc * 128: i * 1024 + (oc + 1) * 128],
                       xb[:, i * OWN + tt * 512: i * OWN + tt * 512 + 512],
                       i == 0, i == 7) for i in range(8)])
                p.v("tensor_copy", ot[:], pa[:, 0:512])
                p.dma(dst[oc * 128:(oc + 1) * 128, tt * 512:tt * 512 + 512],
                      ot[:], q="gpsimd")
    return p


def build_l1att():
    p = Prog()
    qTs_d = p.din("qTs", [D, OWN])
    kT_d = p.din("kT", [D, T])
    vb_d = p.din("vblk", [128, 64, D])
    pm_d = p.din("pmsk", [128, 160, 512], BF16)
    atm = p.dout("atm", [OWN, D])
    kbf = p.dint("kbf", [D, T], BF16)
    vbf = p.dint("vbf", [128, 64, D], BF16)

    qb = p.sb("qb", [128, 8 * OWN], BF16)
    stg = p.sb("stg", [128, OWN], F32)
    kc = p.sb("kc", [128, T], BF16)
    vas = p.sb("vas", [128, 64 * 65], BF16)
    va = vas[:, :].rearrange("p (n c) -> p n c", c=65)
    mk = p.sb("mk", [128, 64 * 512], BF16)
    pp = p.sb("pp", [128, 512], BF16)
    pm = p.sb("pm", [128, 512], BF16)
    rl = p.sb("rl", [128, 1], F32)
    otq = p.sb("otq", [128, 64], F32)
    sps = p.psum("sps", [128, 512], F32)
    ops_ = p.psum("ops", [128, 260], F32)

    p.v("memset", va[:, :, 64:65], 1.0)
    for ch in range(8):
        for sl in range(4):
            p.dma(stg[:], kT_d[ch * 128:(ch + 1) * 128, sl * 2048:(sl + 1) * 2048])
            p.v("tensor_copy", kc[:, sl * 2048:(sl + 1) * 2048], stg[:])
        p.dma(kbf[ch * 128:(ch + 1) * 128, :], kc[:], q="gpsimd")
    for n in range(64):
        p.dma(stg[:, 0:D], vb_d[:, n, :])
        p.v("tensor_copy", kc[:, 0:D], stg[:, 0:D])
        p.dma(vbf[:, n, :], kc[:, 0:D], q="gpsimd")
    for ch in range(8):
        p.dma(stg[:], qTs_d[ch * 128:(ch + 1) * 128, :])
        p.v("tensor_copy", qb[:, ch * OWN:(ch + 1) * OWN], stg[:])
    for ch in range(8):
        p.dma(kc[:], kbf[ch * 128:(ch + 1) * 128, :])
        for c in range(4):
            nk = 16 * (c + 1)
            for h2 in range(2):
                h = ch * 2 + h2
                pb = h2 * 64
                m0 = CLS_OFF[c]
                p.dma(mk[:, 0:nk * 512],
                      pm_d[:, m0:m0 + nk, :].rearrange("p n q -> p (n q)"))
                for k0 in range(0, nk, 8):
                    p.dma(va[:, k0:k0 + 8, 0:64],
                          vbf[:, k0:k0 + 8, h * 64:(h + 1) * 64])
                for kb in range(nk):
                    p.mm([(sps[:], kc[pb:pb + 64, kb * 128:(kb + 1) * 128],
                           qb[pb:pb + 64, ch * OWN + c * 512: ch * OWN + (c + 1) * 512],
                           True, True)])
                    p.s("activation", pp[:], sps[:], AF.Exp, scale=0.125)
                    p.v("tensor_mul", pm[:], pp[:], mk[:, kb * 512:(kb + 1) * 512])
                    p.mm([(ops_[:, qq * 65:(qq + 1) * 65],
                           pm[:, qq * 128:(qq + 1) * 128],
                           va[:, kb, 0:65], kb == 0 and qq == 0,
                           kb == nk - 1) for qq in range(4)])
                for qq in range(4):
                    p.v("reciprocal", rl[:], ops_[:, qq * 65 + 64:qq * 65 + 65])
                    p.v("tensor_scalar_mul", otq[:], ops_[:, qq * 65:qq * 65 + 64],
                        rl[:, 0:1])
                    p.dma(atm[c * 512 + qq * 128: c * 512 + (qq + 1) * 128,
                              h * 64:(h + 1) * 64], otq[:], q="gpsimd")
    return p


def build_moe():
    p = Prog()
    xnT_d = p.din("xnT", [D, OWN])
    mskT_d = p.din("mskT", [8, OWN])
    w1_d = p.din("w1", [8, D, 4096])
    w2_d = p.din("w2", [8, 4096, D])
    yT = p.dout("yT", [D, OWN])

    xb = p.sb("xb", [128, 8 * 1024], BF16)
    stg = p.sb("stg", [128, 2048], F32)
    w1s = p.sb("w1s", [128, 8 * 2048], BF16)
    w2s = p.sb("w2s", [128, 16 * 1024], BF16)
    hT = p.sb("hT", [128, 16 * 512], BF16)
    ysb = p.sb("ysb", [128, 2 * 8 * 512], F32)
    acc = p.sb("acc", [128, 8 * 1024], F32)
    mrow = p.sb("mrow", [1, 1024], F32)
    ones = p.sb("ones", [1, 128], F32)
    mb = p.sb("mb", [128, 1024], F32)
    ttmp = p.sb("ttmp", [128, 512], F32)
    pa = p.psum("pa", [128, 512], F32)

    p.v("memset", ones[:], 1.0)
    for s in range(2):
        for i in range(8):
            p.dma(stg[:, 0:1024], xnT_d[i * 128:(i + 1) * 128,
                                        s * 1024:(s + 1) * 1024])
            p.v("tensor_copy", xb[:, i * 1024:(i + 1) * 1024], stg[:, 0:1024])
        p.v("memset", acc[:], 0.0)
        for e in range(8):
            p.dma(mrow[:], mskT_d[e:e + 1, s * 1024:(s + 1) * 1024])
            for half in range(2):
                p.mm([(pa[:, 0:512], ones[:], mrow[:, half * 512:(half + 1) * 512],
                       True, True)])
                p.v("tensor_copy", mb[:, half * 512:(half + 1) * 512], pa[:, 0:512])
            for fh in range(2):
                for i in range(8):
                    p.dma(stg[:], w1_d[e, i * 128:(i + 1) * 128,
                                       fh * 2048:(fh + 1) * 2048])
                    p.v("tensor_copy", w1s[:, i * 2048:(i + 1) * 2048], stg[:])
                for hc in range(16):
                    p.dma(stg[:, 0:1024], w2_d[e, fh * 2048 + hc * 128:
                                               fh * 2048 + (hc + 1) * 128, :])
                    p.v("tensor_copy", w2s[:, hc * 1024:(hc + 1) * 1024],
                        stg[:, 0:1024])
                for tt in range(2):
                    for of in range(16):
                        p.mm([(pa[:, 0:512],
                               w1s[:, i * 2048 + of * 128: i * 2048 + (of + 1) * 128],
                               xb[:, i * 1024 + tt * 512: i * 1024 + tt * 512 + 512],
                               i == 0, i == 7) for i in range(8)])
                        p.s("activation", hT[:, of * 512:(of + 1) * 512],
                            pa[:, 0:512], AF.Gelu)
                    for oc in range(8):
                        p.mm([(pa[:, 0:512],
                               w2s[:, hc * 1024 + oc * 128: hc * 1024 + (oc + 1) * 128],
                               hT[:, hc * 512:(hc + 1) * 512],
                               hc == 0, hc == 15) for hc in range(16)])
                        dst = ysb[:, tt * 4096 + oc * 512: tt * 4096 + (oc + 1) * 512]
                        if fh == 0:
                            p.v("tensor_copy", dst, pa[:, 0:512])
                        else:
                            p.v("tensor_tensor", dst, dst, pa[:, 0:512], ALU.add)
            for tt in range(2):
                for oc in range(8):
                    p.v("tensor_mul", ttmp[:],
                        ysb[:, tt * 4096 + oc * 512: tt * 4096 + (oc + 1) * 512],
                        mb[:, tt * 512:(tt + 1) * 512])
                    a = acc[:, oc * 1024 + tt * 512: oc * 1024 + tt * 512 + 512]
                    p.v("tensor_tensor", a, a, ttmp[:], ALU.add)
        for oc in range(8):
            p.dma(yT[oc * 128:(oc + 1) * 128, s * 1024:(s + 1) * 1024],
                  acc[:, oc * 1024:(oc + 1) * 1024], q="gpsimd")
    return p


def build_final():
    p = Prog()
    y1_d = p.din("y1", [OWN, D])
    x1_d = p.din("x1", [OWN, D])
    mo_d = p.din("mo", [OWN, D])
    mp_d = p.din("mp", [128, 16])
    gam_d = p.din("gam", [1, D])
    bet_d = p.din("bet", [1, D])
    out_d = p.dout("o", [OWN, D])

    yt = p.sb("yt", [128, D], F32)
    xc = p.sb("xc", [128, D], F32)
    sq = p.sb("sq", [128, D], F32)
    st = p.sb("st", [128, 4], F32)
    cst = p.sb("cst", [128, 4], F32)
    grow = p.sb("grow", [1, D], F32)
    brow = p.sb("brow", [1, D], F32)
    ones = p.sb("ones", [1, 128], F32)
    gb = p.sb("gb", [128, D], F32)
    bb = p.sb("bb", [128, D], F32)
    mps = p.sb("mps", [128, 16], F32)
    mot = p.sb("mot", [128, D], F32)
    g1 = p.sb("g1", [128, D], F32)
    xt1 = p.sb("xt1", [128, D], F32)
    ot = p.sb("ot", [128, D], F32)
    pa = p.psum("pa", [128, 512], F32)

    p.v("memset", cst[:, 0:1], 1.0 / D)
    p.v("memset", cst[:, 1:2], 1e-5)
    p.v("memset", cst[:, 2:3], -0.5)
    p.v("memset", ones[:], 1.0)
    p.dma(grow[:], gam_d[:, :])
    p.dma(brow[:], bet_d[:, :])
    p.dma(mps[:], mp_d[:, :])
    for half in range(2):
        p.mm([(pa[:, 0:512], ones[:], grow[:, half * 512:(half + 1) * 512],
               True, True)])
        p.v("tensor_copy", gb[:, half * 512:(half + 1) * 512], pa[:, 0:512])
        p.mm([(pa[:, 0:512], ones[:], brow[:, half * 512:(half + 1) * 512],
               True, True)])
        p.v("tensor_copy", bb[:, half * 512:(half + 1) * 512], pa[:, 0:512])
    for t in range(OWN // 128):
        sl = slice(t * 128, (t + 1) * 128)
        p.dma(yt[:], y1_d[sl, :])
        _ln_stats(p, yt, xc, sq, st, cst)
        p.v("tensor_mul", xc[:], xc[:], gb[:])
        p.v("tensor_tensor", xc[:], xc[:], bb[:], ALU.add)
        p.dma(mot[:], mo_d[sl, :])
        p.v("tensor_scalar_mul", g1[:], mot[:], mps[:, t:t + 1])
        p.s("activation", g1[:], g1[:], AF.Silu)
        p.dma(xt1[:], x1_d[sl, :])
        p.v("tensor_mul", ot[:], xc[:], g1[:])
        p.v("tensor_tensor", ot[:], ot[:], xt1[:], ALU.add)
        p.dma(out_d[sl, :], ot[:], q="gpsimd")
    return p


_PROGS = {}
_BUILDERS = {
    "ln": build_ln, "l0ffn": build_l0ffn, "l0att": build_l0att,
    "l0cmb": build_l0cmb, "qkv1": build_qkv1, "l1att": build_l1att,
    "proj1": build_proj1, "moe": build_moe, "final": build_final,
}


def _prog(name):
    if name not in _PROGS:
        _PROGS[name] = _BUILDERS[name]().finish()
    return _PROGS[name]


def _run(name, in_maps):
    res = run_bass_kernel_spmd(_prog(name).nc, in_maps, core_ids=list(range(8)))
    return res.results


def _c(a):
    return np.ascontiguousarray(a, dtype=np.float32)


def _cb(a):
    return np.ascontiguousarray(a.astype(NBF))


def _chunk8(vec):
    return _c(vec.reshape(8, 128).T)


def host_l0att_inputs(inputs, xn0T, s, pp):
    lo = pp * OWN - 128
    halo = np.zeros((D, OWN + 256), np.float32)
    a, b = max(lo, 0), min(lo + OWN + 256, T)
    halo[:, a - lo:b - lo] = xn0T[s][:, a:b]
    valid = ((np.arange(OWN + 256) + lo) >= 0) & ((np.arange(OWN + 256) + lo) < T)
    km = _c(np.broadcast_to(valid, (128, OWN + 256)))
    vmask = _c(valid.reshape(18, 128).T)
    kl = np.arange(128)
    ql = np.arange(128)
    band = np.zeros((128, 384), np.float32)
    for kbi in range(3):
        rel = (kbi * 128 + kl[:, None]) - ql[None, :]
        band[:, kbi * 128:(kbi + 1) * 128] = ((rel >= 0) & (rel < 256))
    return {
        "xh": _c(halo),
        "wqp": _c(inputs["L0_attn_Wq"][:, PERM_COLS]),
        "bqp": _chunk8(inputs["L0_attn_bq"][PERM_COLS]),
        "wk": _c(inputs["L0_attn_Wk"]),
        "bk": _c(inputs["L0_attn_bk"][:, None]),
        "wv": _c(inputs["L0_attn_Wv"]),
        "bv": _c(inputs["L0_attn_bv"][None, :]),
        "kmask": km, "vmask": vmask, "band": _cb(band),
    }


def host_l1_masks(pp):
    pm = np.zeros((160, 128, 512), np.float32)
    kl = np.arange(128)[:, None]
    ql = np.arange(512)[None, :]
    for c in range(4):
        q0 = (4 * c + pp) * 512
        for kb in range(16 * (c + 1)):
            pm[CLS_OFF[c] + kb] = (kb * 128 + kl) <= (q0 + ql)
    return _cb(pm.transpose(1, 0, 2))


def kernel(**inputs):
    x = _c(inputs["L0_attn_x"])
    cores = [(c // 4, c % 4) for c in range(8)]

    def own(a3, s, pp):
        return a3[s][pp * OWN:(pp + 1) * OWN]

    r = _run("ln", [{"x": _c(own(x, s, pp))} for s, pp in cores])
    xn0 = np.stack([np.concatenate([r[4 * s + pp]["xn"] for pp in range(4)])
                    for s in range(2)])
    xn0T = [np.ascontiguousarray(xn0[s].T) for s in range(2)]

    wrT = _c(inputs["L0_ffn_W_r"].T)
    wkT = _c(inputs["L0_ffn_W_k"].T)
    wvT = _c(inputs["L0_ffn_W_v"].T)
    mur = _chunk8(inputs["L0_ffn_mu_r"])
    muk = _chunk8(inputs["L0_ffn_mu_k"])
    maps = []
    for s, pp in cores:
        pcol = (np.zeros((128, 8), np.float32) if pp == 0 else
                _c(xn0T[s][:, pp * OWN - 1].reshape(8, 128).T))
        maps.append({"xnT": _c(xn0T[s][:, pp * OWN:(pp + 1) * OWN]),
                     "pcol": pcol, "WrT": wrT, "WkT": wkT, "WvT": wvT,
                     "mur": mur, "muk": muk})
    r = _run("l0ffn", maps)
    g0T = [r[c]["gT"] for c in range(8)]

    r = _run("l0att", [host_l0att_inputs(inputs, xn0T, s, pp)
                       for s, pp in cores])
    a0 = [r[c]["atm"] for c in range(8)]

    wo = _c(inputs["L0_attn_Wo"])
    bo = _chunk8(inputs["L0_attn_bo"])
    maps = []
    for c, (s, pp) in enumerate(cores):
        maps.append({"aT": _c(a0[c].T), "wo": wo, "bo": bo, "gT": g0T[c],
                     "xT": _c(own(x, s, pp).T)})
    r = _run("l0cmb", maps)
    x1T = [r[c]["x1T"] for c in range(8)]
    x1 = [np.ascontiguousarray(x1T[c].T) for c in range(8)]

    r = _run("ln", [{"x": x1[c]} for c in range(8)])
    xn1 = [r[c]["xn"] for c in range(8)]
    xn1T = [np.ascontiguousarray(xn1[c].T) for c in range(8)]

    wq1 = _c(inputs["L1_attn_w_qkv"][:, 0:D])
    wk1 = _c(inputs["L1_attn_w_qkv"][:, D:2 * D])
    wv1 = _c(inputs["L1_attn_w_qkv"][:, 2 * D:3 * D])
    wr = _c(np.concatenate([inputs["L1_ffn_Wr"][i * 128:(i + 1) * 128]
                            for i in range(8)], axis=1))
    br = _c(inputs["L1_ffn_br"][None, :])
    r = _run("qkv1", [{"xnT": xn1T[c], "wq1": wq1, "wk1": wk1, "wv1": wv1,
                       "wr": wr, "br": br} for c in range(8)])
    qT = [r[c]["qT"] for c in range(8)]
    kT = [r[c]["kT"] for c in range(8)]
    vT = [r[c]["vT"] for c in range(8)]
    msk = [r[c]["msk"] for c in range(8)]
    mp = [r[c]["mp"] for c in range(8)]

    maps = []
    for s in range(2):
        kT_seq = np.concatenate([kT[4 * s + pp] for pp in range(4)], axis=1)
        qT_seq = np.concatenate([qT[4 * s + pp] for pp in range(4)], axis=1)
        v_seq = np.concatenate([vT[4 * s + pp].T for pp in range(4)])
        vblk = np.ascontiguousarray(
            v_seq.reshape(64, 128, D).transpose(1, 0, 2))
        for pp in range(4):
            qsel = np.concatenate(
                [qT_seq[:, (4 * c + pp) * 512:(4 * c + pp + 1) * 512]
                 for c in range(4)], axis=1)
            maps.append({"qTs": _c(qsel), "kT": _c(kT_seq), "vblk": _c(vblk),
                         "pmsk": host_l1_masks(pp)})
    r = _run("l1att", maps)
    attn1_seq = [np.zeros((T, D), np.float32) for _ in range(2)]
    for s in range(2):
        for pp in range(4):
            am = r[4 * s + pp]["atm"]
            for c in range(4):
                j = 4 * c + pp
                attn1_seq[s][j * 512:(j + 1) * 512] = am[c * 512:(c + 1) * 512]

    wp = _c(inputs["L1_attn_w_proj"])
    bp = _chunk8(inputs["L1_attn_b_proj"])
    maps = []
    for c, (s, pp) in enumerate(cores):
        aT = _c(attn1_seq[s][pp * OWN:(pp + 1) * OWN].T)
        maps.append({"aT": aT, "wp": wp, "bp": bp, "xnT": xn1T[c]})
    r = _run("proj1", maps)
    y1T = [r[c]["y1T"] for c in range(8)]

    w1 = _c(inputs["L1_ffn_W1"])
    w2 = _c(inputs["L1_ffn_W2"])
    r = _run("moe", [{"xnT": xn1T[c], "mskT": _c(msk[c].T),
                      "w1": w1, "w2": w2} for c in range(8)])
    moT = [r[c]["yT"] for c in range(8)]

    gam = _c(inputs["L1_attn_ln_gamma"][None, :])
    bet = _c(inputs["L1_attn_ln_beta"][None, :])
    maps = []
    for c in range(8):
        maps.append({"y1": _c(y1T[c].T), "x1": x1[c], "mo": _c(moT[c].T),
                     "mp": _c(mp[c].reshape(16, 128).T),
                     "gam": gam, "bet": bet})
    r = _run("final", maps)
    out = np.zeros((2, T, D), np.float32)
    for c, (s, pp) in enumerate(cores):
        out[s, pp * OWN:(pp + 1) * OWN] = r[c]["o"]
    return out
